# version 29
# baseline (speedup 1.0000x reference)
import numpy as np
import ml_dtypes as _mld
from contextlib import ExitStack
_bf16 = _mld.bfloat16

import concourse.bass as bass
import concourse.mybir as mybir
from concourse.bass_utils import run_bass_kernel_spmd

F32 = mybir.dt.float32
BF16 = mybir.dt.bfloat16
OP = mybir.AluOpType
AF = mybir.ActivationFunctionType
AX = mybir.AxisListType.X
ts = bass.ts

C = 1024
G = 8
R = 8192
OWN = 2048
RT = 256
NRT = R // RT
OWN_RT0 = (R - OWN) // RT
HALF = 1024
E = 8
INTER = 1024
SH = 2048


class Sem:
    def __init__(self, ctx, nc, name):
        self.h = ctx.enter_context(nc.semaphore(name))
        self.n = 0

    def inc(self, instr, k=1):
        instr.then_inc(self.h, k)
        self.n += k
        return instr

    def incd(self, instr):
        return self.inc(instr, 16)

    def wait(self, eng, val=None):
        eng.wait_ge(self.h, self.n if val is None else val)


def build():
    nc = bass.Bass("TRN2", target_bir_lowering=False)
    di = lambda n, s: nc.dram_tensor(n, s, F32, kind="ExternalInput").ap()
    xs = di("xs", [128, NRT, G, RT])
    xt = di("xt", [OWN, C])
    t0c = di("t0c", [128, 1])
    vwT = di("vwT", [128, G, C]); rwT = di("rwT", [128, G, C]); owT = di("owT", [128, G, C])
    gwT = di("gwT", [C, E])
    w1T = di("w1T", [E, 128, G, INTER]); w3T = di("w3T", [E, 128, G, INTER]); w2T = di("w2T", [E, 128, G, C])
    ws1T = di("ws1T", [2, 128, G, INTER]); ws3T = di("ws3T", [2, 128, G, INTER]); ws2T = di("ws2T", [2, 128, G, C])
    b1h = di("b1h", [128, E, G]); b3h = di("b3h", [128, E, G])
    b2g = di("b2g", [E + 1, C])
    bs1h = di("bs1h", [128, 2, G]); bs3h = di("bs3h", [128, 2, G])
    gbc = di("gbc", [E, 1])
    tdh = di("tdh", [128, G]); tfh = di("tfh", [128, G])
    idf = di("idf", [128, 128])
    iotr = di("iotr", [128, RT])
    out = nc.dram_tensor("out", [OWN, C], F32, kind="ExternalOutput").ap()

    with ExitStack() as ctx:
        sb = lambda n, s, d: ctx.enter_context(nc.sbuf_tensor(n, s, d))
        psum = lambda n, s: ctx.enter_context(nc.psum_tensor(n, s, F32))

        dS = Sem(ctx, nc, "dS")
        gD = Sem(ctx, nc, "gD")
        tS = Sem(ctx, nc, "tS")
        vS = Sem(ctx, nc, "vS")
        sS = Sem(ctx, nc, "sS")
        gS = Sem(ctx, nc, "gS")
        oS = Sem(ctx, nc, "oS")
        sems = [dS, gD, tS, vS, sS, gS, oS]

        for s_ in sems:
            nc.gpsimd.sem_clear(s_.h)
        nc.all_engine_barrier()

        lamc = sb("lamc", [128, G], F32)
        g1c = sb("g1c", [128, G], F32)
        e2c = sb("e2c", [128, G], F32)
        e2pc = sb("e2pc", [128, G], F32)
        b0c = sb("b0c", [128, G], F32)
        zeroc = sb("zeroc", [128, G], F32)
        onec = sb("onec", [128, 1], F32)
        epsc = sb("epsc", [128, 1], F32)
        ones_bf = sb("ones_bf", [128, 128], BF16)
        onesr_f = sb("onesr_f", [1, 128], F32)
        gbs = sb("gbs", [E, 1], F32)
        t0s = sb("t0s", [128, 1], F32)
        b1s = sb("b1s", [128, E, G], F32)
        b3s = sb("b3s", [128, E, G], F32)
        bs1s = sb("bs1s", [128, 2, G], F32)
        bs3s = sb("bs3s", [128, 2, G], F32)
        tdc = sb("tdc", [128, G], F32)
        tfc = sb("tfc", [128, G], F32)
        a_ = sb("a_", [128, G], F32)
        gx = sb("gx", [128, G], F32)
        scr = sb("scr", [128, G], F32)
        scr2 = sb("scr2", [128, G], F32)
        scr3 = sb("scr3", [128, G], F32)
        scr4 = sb("scr4", [128, G], F32)

        xn_own = sb("xn_own", [128, 8, G, RT], BF16)
        rwkv = sb("rwkv", [128, 8, G, RT], BF16)
        gtok = sb("gtok", [128, 16, E], F32)
        g9 = sb("g9", [E + 1, OWN], BF16)
        gws = sb("gws", [128, G, E], BF16)

        p1ctx = ExitStack()
        sb1 = lambda n, s, d: p1ctx.enter_context(nc.sbuf_tensor(n, s, d))
        vws = sb1("vws", [128, G, C], BF16)
        rws = sb1("rws", [128, G, C], BF16)

        ps = [psum(f"ps{i}", [128, 512]) for i in range(8)]
        bank_free = {}

        def bank_wait(i):
            if i in bank_free:
                s_, v_ = bank_free[i]
                nc.tensor.wait_ge(s_.h, v_)

        dS.incd(nc.sync.dma_start(out=tdc[:], in_=tdh[:, :]))
        dS.incd(nc.sync.dma_start(out=tfc[:], in_=tfh[:, :]))
        dS.incd(nc.sync.dma_start(out=t0s[:], in_=t0c[:, :]))
        dS.incd(nc.sync.dma_start(out=gbs[:], in_=gbc[:, :]))
        dS.incd(nc.sync.dma_start(out=b1s[:], in_=b1h[:, :, :]))
        dS.incd(nc.sync.dma_start(out=b3s[:], in_=b3h[:, :, :]))
        dS.incd(nc.sync.dma_start(out=bs1s[:], in_=bs1h[:, :, :]))
        dS.incd(nc.sync.dma_start(out=bs3s[:], in_=bs3h[:, :, :]))
        gD.incd(nc.gpsimd.dma_start(out=vws[:], in_=vwT[:, :, :]))
        gD.incd(nc.gpsimd.dma_start(out=rws[:], in_=rwT[:, :, :]))
        gD.incd(nc.gpsimd.dma_start(out=gws[:], in_=gwT.rearrange("(g p) e -> p g e", p=128)))

        nc.vector.wait_ge(dS.h, dS.n)
        nc.scalar.wait_ge(dS.h, dS.n)
        vS.inc(nc.vector.memset(zeroc[:], 0.0))
        vS.inc(nc.vector.memset(onec[:], 1.0))
        vS.inc(nc.vector.memset(epsc[:], 1e-5))
        vS.inc(nc.vector.memset(ones_bf[:], 1.0 / 1024.0))
        vS.inc(nc.vector.memset(onesr_f[:], 1.0))
        sS.inc(nc.scalar.activation(a_[:], tdc[:], AF.Relu))
        sS.inc(nc.scalar.activation(lamc[:], a_[:], AF.Exp, scale=-1.0))
        sS.inc(nc.scalar.activation(scr[:], tdc[:], AF.Relu, scale=-1.0))
        sS.inc(nc.scalar.activation(scr[:], scr[:], AF.Exp, scale=-1.0))
        sS.inc(nc.scalar.activation(scr2[:], tfc[:], AF.Relu))
        sS.inc(nc.scalar.activation(scr2[:], scr2[:], AF.Exp, scale=-1.0))
        sS.inc(nc.scalar.activation(e2c[:], tfc[:], AF.Relu, scale=-1.0))
        sS.inc(nc.scalar.activation(e2c[:], e2c[:], AF.Exp, scale=-1.0))
        sS.wait(nc.vector)
        vS.inc(nc.vector.tensor_mul(g1c[:], scr2[:], scr[:]))
        vS.inc(nc.vector.tensor_scalar(e2pc[:], e2c[:], 1.0, 1e-6, OP.mult, OP.add))

        TAU = 1.0 / 64.0
        BIG = float(2 ** 20)

        def g_of(dst, y, tmp1, tmp2):
            vS.inc(nc.vector.tensor_scalar(tmp1[:], y[:], 1.0 / 6.0, -0.5, OP.mult, OP.add))
            vS.inc(nc.vector.scalar_tensor_tensor(tmp1[:], y[:], 1.0, tmp1[:], OP.mult, OP.mult))
            vS.inc(nc.vector.tensor_scalar(tmp1[:], tmp1[:], 1.0, 1.0, OP.mult, OP.add))
            vS.inc(nc.vector.tensor_scalar(tmp2[:], y[:], TAU / 2.0, TAU / 2.0, OP.max, OP.max))
            vS.inc(nc.vector.reciprocal(dst[:], tmp2[:]))
            vS.wait(nc.scalar)
            sS.inc(nc.scalar.activation(tmp2[:], tmp2[:], AF.Exp, scale=-1.0))
            sS.wait(nc.vector)
            vS.inc(nc.vector.tensor_scalar(tmp2[:], tmp2[:], -1.0, 1.0, OP.mult, OP.add))
            vS.inc(nc.vector.tensor_mul(dst[:], dst[:], tmp2[:]))
            vS.inc(nc.vector.tensor_scalar(tmp2[:], y[:], -BIG, TAU * BIG, OP.mult, OP.add))
            vS.inc(nc.vector.tensor_scalar(tmp2[:], tmp2[:], 0.0, 1.0, OP.max, OP.min))
            vS.inc(nc.vector.tensor_sub(tmp1[:], tmp1[:], dst[:]))
            vS.inc(nc.vector.tensor_mul(tmp1[:], tmp1[:], tmp2[:]))
            vS.inc(nc.vector.tensor_add(dst[:], dst[:], tmp1[:]))

        sS.wait(nc.vector)
        vS.inc(nc.vector.tensor_scalar_mul(scr3[:], a_[:], t0s[:, 0:1]))
        g_of(scr4, scr3, scr, scr2)
        vS.inc(nc.vector.tensor_copy(gx[:], scr4[:]))
        g_of(scr4, a_, scr, scr2)
        vS.inc(nc.vector.reciprocal(scr4[:], scr4[:]))
        vS.inc(nc.vector.tensor_mul(gx[:], gx[:], scr4[:]))
        vS.inc(nc.vector.tensor_scalar_mul(b0c[:], gx[:], t0s[:, 0:1]))

        nc.all_engine_barrier()

        xf = [sb1(f"xf{i}", [128, G, RT], F32) for i in range(2)]
        xsq = [sb1(f"xsq{i}", [128, 2, G, RT], BF16) for i in range(2)]
        v_sb = [sb1(f"v_sb{i}", [128, G, RT], F32) for i in range(2)]
        r_sb = sb1("r_sb", [128, G, RT], F32)
        aa = sb1("aa", [128, G, RT + 1], F32)
        bb = sb1("bb", [128, G, RT + 1], F32)
        aaS = sb1("aaS", [128, G], F32)
        bbS = sb1("bbS", [128, G], F32)
        num = sb1("num", [128, G, RT], F32)
        den = sb1("den", [128, G, RT], F32)
        wlam = sb1("wlam", [128, G, RT], F32)
        lamRT = sb1("lamRT", [128, G], F32)
        nega = sb1("nega", [128, G], F32)
        ctrb = sb1("ctrb", [128, G], F32)
        is_ = [sb1(f"is{i}", [128, RT], F32) for i in range(3)]
        msi = [sb1(f"msi{i}", [128, RT], F32) for i in range(3)]

        vS.inc(nc.vector.tensor_copy(aaS[:], zeroc[:]))
        vS.inc(nc.vector.tensor_copy(bbS[:], b0c[:]))
        dS.incd(nc.sync.dma_start(out=is_[0][:], in_=iotr[:, :]))
        vS.inc(nc.vector.tensor_scalar(nega[:], a_[:], -1.0, 0.0, OP.mult, OP.add))
        nc.scalar.wait_ge(dS.h, dS.n)
        sS.wait(nc.scalar)
        vS.wait(nc.scalar)
        for g in range(G):
            sS.inc(nc.scalar.activation(wlam[:, g, :], is_[0][:], AF.Exp, scale=nega[:, g:g + 1]))
        sS.inc(nc.scalar.activation(lamRT[:], a_[:], AF.Exp, scale=-float(RT)))
        sS.wait(nc.vector)

        xf_free = [None, None]
        guards = {}
        LN = {}
        pend_rwkv = []

        def emit_LN(rt):
            par = rt % 3
            fpar = rt % 2
            xpar = rt % 2
            own = rt >= OWN_RT0
            o0 = (rt - OWN_RT0) * RT
            if xf_free[fpar] is not None:
                s_, v_ = xf_free[fpar]
                nc.sync.wait_ge(s_.h, v_)
            dS.incd(nc.sync.dma_start(out=xf[fpar][:], in_=xs[:, rt]))
            dma_done = dS.n
            nc.gpsimd.wait_ge(dS.h, dma_done)
            if ("stats_mm", xpar) in guards:
                nc.gpsimd.wait_ge(tS.h, guards[("stats_mm", xpar)])
            gD.incd(nc.gpsimd.dma_start(out=xsq[xpar][:, 0], in_=xf[fpar][:]))
            cast_done = gD.n
            nc.scalar.wait_ge(dS.h, dma_done)
            if ("stats_mm", xpar) in guards:
                nc.scalar.wait_ge(tS.h, guards[("stats_mm", xpar)])
            sS.inc(nc.scalar.activation(xsq[xpar][:, 1], xf[fpar][:], AF.Square))
            sq_done = sS.n
            bank_wait(4 + par)
            nc.tensor.wait_ge(gD.h, cast_done)
            nc.tensor.wait_ge(sS.h, sq_done)
            for g in range(G):
                mm = nc.tensor.matmul(ps[4 + par][:, 0:2 * RT], ones_bf[:], xsq[xpar][:, :, g, :],
                                      start=(g == 0), stop=(g == G - 1))
            tS.inc(mm)
            guards[("stats_mm", xpar)] = tS.n
            bcast_done = tS.n
            if ("u_mult", par) in guards:
                nc.scalar.wait_ge(vS.h, guards[("u_mult", par)])
            nc.scalar.wait_ge(tS.h, bcast_done)
            sS.inc(nc.scalar.activation(is_[par][:], ps[4 + par][:, 0:RT], AF.Square))
            msq_done = sS.n
            nc.vector.wait_ge(sS.h, msq_done)
            vS.inc(nc.vector.tensor_sub(msi[par][:], ps[4 + par][:, RT:2 * RT], is_[par][:]))
            var_done = vS.n
            nc.scalar.wait_ge(vS.h, var_done)
            sS.inc(nc.scalar.activation(is_[par][:], msi[par][:], AF.Ln, bias=epsc[:, 0:1]))
            sS.inc(nc.scalar.activation(is_[par][:], is_[par][:], AF.Exp, scale=-0.5))
            is_done = sS.n
            nc.vector.wait_ge(sS.h, is_done)
            vS.inc(nc.vector.tensor_mul(msi[par][:], ps[4 + par][:, 0:RT], is_[par][:]))
            bank_free[4 + par] = (vS, vS.n)
            vS.inc(nc.vector.tensor_mul(
                xf[fpar][:], xf[fpar][:],
                is_[par].rearrange("p (u t) -> p u t", u=1).broadcast_to([128, G, RT])))
            vS.inc(nc.vector.tensor_sub(
                xf[fpar][:], xf[fpar][:],
                msi[par].rearrange("p (u t) -> p u t", u=1).broadcast_to([128, G, RT])))
            guards[("u_mult", par)] = vS.n
            guards[("xn", par)] = vS.n
            u_done = vS.n
            slot = rt % 8
            xn_rt = xn_own[:, slot]
            nc.gpsimd.wait_ge(vS.h, u_done)
            if ("xn_slot", slot) in guards:
                nc.gpsimd.wait_ge(tS.h, guards[("xn_slot", slot)])
            gD.incd(nc.gpsimd.dma_start(out=xn_rt, in_=xf[fpar][:]))
            xf_free[fpar] = (gD, gD.n)
            LN[rt] = dict(xn=(gD, gD.n), xn_rt=xn_rt)

        def emit_C(rt):
            par = rt % 2
            own = rt >= OWN_RT0
            o0 = (rt - OWN_RT0) * RT
            xn_rt = LN[rt]["xn_rt"]
            while pend_rwkv:
                v_, oo = pend_rwkv.pop(0)
                nc.gpsimd.wait_ge(vS.h, v_)
                gD.incd(nc.gpsimd.dma_start(out=rwkv[:, oo // RT], in_=num[:]))
                guards["rwkv_cast"] = gD.n
            xnsem, xnval = LN[rt]["xn"]
            nc.tensor.wait_ge(xnsem.h, xnval)
            for dg in range(G):
                b = dg % 4
                bank_wait(b)
                for kg in range(G):
                    mm = nc.tensor.matmul(ps[b][:, 0:RT], vws[:, kg, ts(dg, 128)], xn_rt[:, kg, :],
                                          start=(kg == 0), stop=(kg == G - 1))
                tS.inc(mm)
                if ("scan", par) in guards:
                    nc.scalar.wait_ge(vS.h, guards[("scan", par)])
                tS.wait(nc.scalar)
                sS.inc(nc.scalar.copy(v_sb[par][:, dg, :], ps[b][:, 0:RT]))
                bank_free[b] = (sS, sS.n)
            v_evac_done = sS.n
            guards[("xn_slot", rt % 8)] = tS.n
            nc.vector.wait_ge(sS.h, v_evac_done)
            if own:
                for g in range(G):
                    vS.inc(nc.vector.tensor_copy(aa[:, g, 0:1], aaS[:, g:g + 1]))
                for g in range(G):
                    vS.inc(nc.vector.tensor_tensor_scan(
                        aa[:, g, 1:RT + 1], lamc[:, g:g + 1].broadcast_to([128, RT]), v_sb[par][:, g, :],
                        aaS[:, g:g + 1], OP.mult, OP.add))
                for g in range(G):
                    vS.inc(nc.vector.tensor_copy(aaS[:, g:g + 1], aa[:, g, RT:RT + 1]))
            else:
                for g in range(G):
                    vS.inc(nc.vector.scalar_tensor_tensor(
                        aa[:, g, 0:RT], v_sb[par][:, g, :], 1.0, wlam[:, g, :],
                        OP.mult, OP.mult, accum_out=ctrb[:, g:g + 1]))
                vS.inc(nc.vector.tensor_mul(aaS[:], aaS[:], lamRT[:]))
                vS.inc(nc.vector.tensor_add(aaS[:], aaS[:], ctrb[:]))
            guards[("scan", par)] = vS.n
            if own:
                for g in range(G):
                    vS.inc(nc.vector.tensor_copy(bb[:, g, 0:1], bbS[:, g:g + 1]))
                for g in range(G):
                    vS.inc(nc.vector.tensor_tensor_scan(
                        bb[:, g, 1:RT + 1], lamc[:, g:g + 1].broadcast_to([128, RT]),
                        onec[:, 0:1].broadcast_to([128, RT]), bbS[:, g:g + 1], OP.mult, OP.add))
                for g in range(G):
                    vS.inc(nc.vector.tensor_copy(bbS[:, g:g + 1], bb[:, g, RT:RT + 1]))
                guards[("scan", par)] = vS.n
                for dg in range(G):
                    b = dg % 4
                    bank_wait(b)
                    for kg in range(G):
                        mm = nc.tensor.matmul(ps[b][:, 0:RT], rws[:, kg, ts(dg, 128)],
                                              xn_own[:, rt - OWN_RT0, kg, :],
                                              start=(kg == 0), stop=(kg == G - 1))
                    tS.inc(mm)
                    if "rmult" in guards:
                        nc.scalar.wait_ge(vS.h, guards["rmult"])
                    tS.wait(nc.scalar)
                    sS.inc(nc.scalar.activation(r_sb[:, dg, :], ps[b][:, 0:RT], AF.Exp, scale=-1.0))
                    bank_free[b] = (sS, sS.n)
                sS.inc(nc.scalar.activation(r_sb[:], r_sb[:], AF.Ln, bias=onec[:, 0:1]))
                sS.inc(nc.scalar.activation(r_sb[:], r_sb[:], AF.Exp, scale=-1.0))
                r_done = sS.n
                if "wkv_mult" in guards:
                    nc.scalar.wait_ge(vS.h, guards["wkv_mult"])
                for g in range(G):
                    sS.inc(nc.scalar.mul(den[:, g, :], v_sb[par][:, g, :], e2c[:, g:g + 1]))
                e2v_done = sS.n
                if "rwkv_cast" in guards:
                    nc.vector.wait_ge(gD.h, guards["rwkv_cast"])
                nc.vector.wait_ge(sS.h, e2v_done)
                for g in range(G):
                    vS.inc(nc.vector.scalar_tensor_tensor(
                        num[:, g, :], aa[:, g, 0:RT], g1c[:, g:g + 1], den[:, g, :], OP.mult, OP.add))
                    vS.inc(nc.vector.tensor_scalar(
                        den[:, g, :], bb[:, g, 0:RT], g1c[:, g:g + 1], e2pc[:, g:g + 1], OP.mult, OP.add))
                den_done = vS.n
                nc.scalar.wait_ge(vS.h, den_done)
                sS.inc(nc.scalar.activation(den[:], den[:], AF.Ln))
                sS.inc(nc.scalar.activation(den[:], den[:], AF.Exp, scale=-1.0))
                rden_done = sS.n
                nc.vector.wait_ge(sS.h, rden_done)
                nc.vector.wait_ge(sS.h, r_done)
                vS.inc(nc.vector.tensor_mul(num[:], num[:], den[:]))
                guards["wkv_mult"] = vS.n
                vS.inc(nc.vector.tensor_mul(num[:], num[:], r_sb[:]))
                guards["rmult"] = vS.n
                pend_rwkv.append((vS.n, o0))

        emit_LN(0)
        emit_LN(1)
        for rt in range(NRT):
            if rt + 2 < NRT:
                emit_LN(rt + 2)
            emit_C(rt)
        while pend_rwkv:
            v_, oo = pend_rwkv.pop(0)
            nc.gpsimd.wait_ge(vS.h, v_)
            gD.incd(nc.gpsimd.dma_start(out=rwkv[:, oo // RT], in_=num[:]))
        nc.gpsimd.wait_ge(gD.h, gD.n)

        nc.all_engine_barrier()
        bank_free.clear()
        p1ctx.close()

        ows = sb("ows", [128, G, C], BF16)
        b2gs = sb("b2gs", [E + 1, C], BF16)
        gD.incd(nc.gpsimd.dma_start(out=ows[:], in_=owT[:, :, :]))
        gD.incd(nc.gpsimd.dma_start(out=b2gs[:], in_=b2g[:, :]))

        y_sb = sb("y_sb", [128, G, C], F32)
        w1s = sb("w1s", [128, G, INTER], BF16)
        w3s = sb("w3s", [128, G, INTER], BF16)
        w2s = sb("w2s", [128, G, C], BF16)
        w1r = [w1T[e] for e in range(E)]
        w3r = [w3T[e] for e in range(E)]
        w2r = [w2T[e] for e in range(E)]
        gD.incd(nc.gpsimd.dma_start(out=w1s[:], in_=w1r[0]))
        gD.incd(nc.gpsimd.dma_start(out=w3s[:], in_=w3r[0]))
        gD.incd(nc.gpsimd.dma_start(out=w2s[:], in_=w2r[0]))
        guards["p2w"] = gD.n

        gctx = ExitStack()
        sbg = lambda n, s, d: gctx.enter_context(nc.sbuf_tensor(n, s, d))
        idsb = sbg("idsb", [128, 128], F32)
        dS.incd(nc.sync.dma_start(out=idsb[:], in_=idf[:, :]))
        nc.tensor.wait_ge(dS.h, dS.n)
        lgf = sbg("lgf", [E, OWN], F32)
        pall = sbg("pall", [128, 16, E], F32)
        racc = sbg("racc", [128, 16, 32], F32)
        rsum = sbg("rsum", [128, 16], F32)
        colmx = sbg("colmx", [128, 1], F32)
        lgt_sb = sbg("lgt_sb", [128, E], F32)
        colsum = sbg("colsum", [128, 1], F32)
        m2t = sbg("m2t", [128, 16], F32)
        dsc = sbg("dsc", [128, 16, E], F32)

        for nchunk in range(4):
            b = nchunk % 2
            bank_wait(b)
            for kg in range(G):
                mm = nc.tensor.matmul(ps[b][0:E, 0:512], gws[:, kg, :],
                                      xn_own[:, 2 * nchunk:2 * nchunk + 2, kg, :],
                                      start=(kg == 0), stop=(kg == G - 1))
            tS.inc(mm)
            tS.wait(nc.scalar)
            sS.inc(nc.scalar.activation(lgf[:, nchunk * 512:(nchunk + 1) * 512], ps[b][0:E, 0:512],
                                        AF.Identity, bias=gbs[:, 0:1]))
            bank_free[b] = (sS, sS.n)
        for tcg in range(16):
            b = 2 + tcg % 2
            bank_wait(b)
            sS.wait(nc.tensor)
            tS.inc(nc.tensor.transpose(ps[b][:, 0:E], lgf[:, tcg * 128:(tcg + 1) * 128], idsb[0:E, 0:E]))
            tS.wait(nc.scalar)
            sS.inc(nc.scalar.copy(lgt_sb[:], ps[b][:, 0:E]))
            sS.wait(nc.vector)
            vS.inc(nc.vector.reduce_max(colmx[:], lgt_sb[:], axis=AX, negate=True))
            vS.wait(nc.scalar)
            sS.inc(nc.scalar.activation(pall[:, tcg, :], ps[b][:, 0:E], AF.Exp,
                                        bias=colmx[:, 0:1], accum_out=colsum[:, 0:1]))
            bank_free[b] = (sS, sS.n)
            sS.wait(nc.vector)
            vS.inc(nc.vector.reciprocal(rsum[:, tcg:tcg + 1], colsum[:, 0:1]))
        for k in range(1, 5):
            vS.inc(nc.vector.tensor_tensor(racc[:, :, (k - 1) * 8:(k - 1) * 8 + (8 - k)],
                                           pall[:, :, 0:8 - k], pall[:, :, k:8], OP.min))
            vS.inc(nc.vector.tensor_tensor(racc[:, :, (k - 1) * 8 + (8 - k):k * 8],
                                           pall[:, :, 8 - k:8], pall[:, :, 0:k], OP.min))
        vS.inc(nc.vector.reduce_max(m2t.rearrange("p (c u) -> p c u", u=1), racc[:], axis=AX))
        vS.inc(nc.vector.memset(g9[:, :], 1.0))
        vS.inc(nc.vector.tensor_sub(dsc[:], pall[:],
                                    m2t.rearrange("p (c u) -> p c u", u=1).broadcast_to([128, 16, E])))
        vS.inc(nc.vector.tensor_scalar(dsc[:], dsc[:], 1.0, 1e-12, OP.mult, OP.add))
        vS.wait(nc.scalar)
        sS.inc(nc.scalar.sign(dsc[:], dsc[:]))
        sS.wait(nc.vector)
        vS.inc(nc.vector.tensor_scalar(dsc[:], dsc[:], 0.0, 0.0, OP.max, OP.max))
        vS.inc(nc.vector.tensor_mul(gtok[:], pall[:], dsc[:]))
        vS.inc(nc.vector.tensor_mul(gtok[:], gtok[:],
                                    rsum.rearrange("p (c u) -> p c u", u=1).broadcast_to([128, 16, E])))
        vS.wait(nc.scalar)
        for tcg in range(16):
            b = 2 + tcg % 2
            bank_wait(b)
            vS.wait(nc.tensor)
            tS.inc(nc.tensor.transpose(ps[b][0:E, 0:128], gtok[:, tcg, :], idsb[:, :]))
            tS.wait(nc.scalar)
            sS.inc(nc.scalar.copy(g9[0:E, tcg * 128:(tcg + 1) * 128], ps[b][0:E, 0:128]))
            bank_free[b] = (sS, sS.n)

        nc.all_engine_barrier()
        bank_free.clear()
        gctx.close()


        xtr = xt.rearrange("(h c p) d -> h p c d", p=128, c=G)
        outr = out.rearrange("(h c p) d -> h p c d", p=128, c=G)

        ws1r = [ws1T[k] for k in range(2)]
        ws3r = [ws3T[k] for k in range(2)]
        ws2r = [ws2T[k] for k in range(2)]

        contribs = [("exp", e) for e in range(E)] + [("sh", 0), ("sh", 1)]
        w13_ready = guards.pop("p2w", gD.n)
        w13_lo = w13_ready
        w2_ready = w13_ready
        slot_free = {0: 0, 1: 0}

        def tile_iter(wave):
            for i in range(8):
                yield i, wave * 4 + i // 2, i % 2

        for half in range(2):
            tbase = half * HALF
            if half == 1:
                nc.sync.wait_ge(tS.h, tS.n)
                nc.vector.wait_ge(oS.h, oS_done)
            hctx = ExitStack()
            xts = hctx.enter_context(nc.sbuf_tensor(f"xts{half}", [128, G, C], F32))
            if half == 1:
                nc.gpsimd.wait_ge(tS.h, tS.n)
            gD.incd(nc.gpsimd.dma_start(out=xts[:], in_=xtr[half]))

            nc.tensor.wait_ge(gD.h, max(w13_ready, gD.n))
            for wave in range(2):
                for i, tc, dc in tile_iter(wave):
                    bank_wait(i)
                    tS.inc(nc.tensor.matmul(ps[i][:, :], g9[:, tbase + tc * 128:tbase + (tc + 1) * 128],
                                            b2gs[:, dc * 512:(dc + 1) * 512], start=True, stop=True))
                tS.wait(nc.vector)
                for i, tc, dc in tile_iter(wave):
                    vS.inc(nc.vector.tensor_add(y_sb[:, tc, dc * 512:(dc + 1) * 512],
                                                ps[i][:, :], xts[:, tc, dc * 512:(dc + 1) * 512]))
                    bank_free[i] = (vS, vS.n)

            for wave in range(2):
                for i, tc, dc in tile_iter(wave):
                    bank_wait(i)
                    gtc = half * 8 + tc
                    for kg in range(G):
                        mm = nc.tensor.matmul(ps[i][:, :], rwkv[:, gtc // 2, kg, (gtc % 2) * 128:(gtc % 2 + 1) * 128],
                                              ows[:, kg, dc * 512:(dc + 1) * 512],
                                              start=(kg == 0), stop=(kg == G - 1))
                    tS.inc(mm)
                tS.wait(nc.vector)
                for i, tc, dc in tile_iter(wave):
                    ysl = y_sb[:, tc, dc * 512:(dc + 1) * 512]
                    vS.inc(nc.vector.tensor_tensor(ysl, ps[i][:, :], ysl, OP.add))
                    bank_free[i] = (vS, vS.n)

            hctx.close()
            hctx = ExitStack()
            h_bf = hctx.enter_context(nc.sbuf_tensor(f"h_bf{half}", [128, G, HALF], BF16))
            a1s = hctx.enter_context(nc.sbuf_tensor(f"a1s{half}", [128, 2, 512], F32))
            a3s = hctx.enter_context(nc.sbuf_tensor(f"a3s{half}", [128, 2, 512], F32))
            ytile = {}
            for ci, (kind, e) in enumerate(contribs):
                for ig in range(G):
                    if ig == 0:
                        nc.tensor.wait_ge(gD.h, w13_lo)
                    if ig == 4:
                        nc.tensor.wait_ge(gD.h, w13_ready)
                    for tch in range(2):
                        b1_ = tch
                        b3_ = 2 + tch
                        bank_wait(b1_)
                        bank_wait(b3_)
                        sl0 = 4 * half + 2 * tch
                        for kg in range(G):
                            nc.tensor.matmul(ps[b1_][:, :], w1s[:, kg, ts(ig, 128)],
                                             xn_own[:, sl0:sl0 + 2, kg, :],
                                             start=(kg == 0), stop=(kg == G - 1))
                        for kg in range(G):
                            mm = nc.tensor.matmul(ps[b3_][:, :], w3s[:, kg, ts(ig, 128)],
                                                  xn_own[:, sl0:sl0 + 2, kg, :],
                                                  start=(kg == 0), stop=(kg == G - 1))
                        tS.inc(mm)
                        if kind == "exp":
                            bc1 = b1s[:, e, ig:ig + 1]
                            bc3 = b3s[:, e, ig:ig + 1]
                        else:
                            bc1 = bs1s[:, e, ig:ig + 1]
                            bc3 = bs3s[:, e, ig:ig + 1]
                        nc.scalar.wait_ge(gS.h, slot_free[tch])
                        tS.wait(nc.scalar)
                        sS.inc(nc.scalar.activation(a1s[:, tch, :], ps[b1_][:, :], AF.Silu, bias=bc1))
                        sS.inc(nc.scalar.activation(a3s[:, tch, :], ps[b3_][:, :], AF.Identity, bias=bc3))
                        bank_free[b1_] = (sS, sS.n)
                        bank_free[b3_] = (sS, sS.n)
                        sS.wait(nc.gpsimd)
                        gS.inc(nc.gpsimd.tensor_tensor(h_bf[:, ig, tch * 512:(tch + 1) * 512],
                                                       a1s[:, tch, :], a3s[:, tch, :], OP.mult))
                        slot_free[tch] = gS.n
                nsrc = None
                if ci + 1 < len(contribs):
                    nk, ne = contribs[ci + 1]
                    nsrc = (w1r[ne] if nk == "exp" else ws1r[ne],
                            w3r[ne] if nk == "exp" else ws3r[ne])
                elif half == 0:
                    nsrc = (w1r[0], w3r[0])
                if nsrc is not None:
                    tS.wait(nc.gpsimd)
                    gD.incd(nc.gpsimd.dma_start(out=w1s[:, :, 0:512], in_=nsrc[0][:, :, 0:512]))
                    gD.incd(nc.gpsimd.dma_start(out=w3s[:, :, 0:512], in_=nsrc[1][:, :, 0:512]))
                    w13_lo = gD.n
                    gD.incd(nc.gpsimd.dma_start(out=w1s[:, :, 512:1024], in_=nsrc[0][:, :, 512:1024]))
                    gD.incd(nc.gpsimd.dma_start(out=w3s[:, :, 512:1024], in_=nsrc[1][:, :, 512:1024]))
                    w13_ready = gD.n
                nc.tensor.wait_ge(gS.h, gS.n)
                nc.tensor.wait_ge(gD.h, w2_ready)
                for wave in range(2):
                    for i, tc, dc in tile_iter(wave):
                        bank_wait(i)
                        for kg in range(G):
                            mm = nc.tensor.matmul(ps[i][:, :], h_bf[:, kg, tc * 128:(tc + 1) * 128],
                                                  w2s[:, kg, dc * 512:(dc + 1) * 512],
                                                  start=(kg == 0), stop=(kg == G - 1))
                        tS.inc(mm)
                    tS.wait(nc.vector)
                    for i, tc, dc in tile_iter(wave):
                        ysl = y_sb[:, tc, dc * 512:(dc + 1) * 512]
                        if kind == "exp":
                            gcol = gtok[:, half * 8 + tc, e:e + 1]
                            vS.inc(nc.vector.scalar_tensor_tensor(ysl, ps[i][:, :], gcol, ysl, OP.mult, OP.add))
                        else:
                            vS.inc(nc.vector.tensor_tensor(ysl, ps[i][:, :], ysl, OP.add))
                        bank_free[i] = (vS, vS.n)
                        if ci == len(contribs) - 1:
                            ytile[(tc, dc)] = vS.n
                tS.wait(nc.gpsimd)
                if ci + 1 < len(contribs):
                    nk, ne = contribs[ci + 1]
                    gD.incd(nc.gpsimd.dma_start(out=w2s[:], in_=(w2r[ne] if nk == "exp" else ws2r[ne])))
                elif half == 0:
                    gD.incd(nc.gpsimd.dma_start(out=w2s[:], in_=w2r[0]))
                w2_ready = gD.n

            hctx.close()
            for tc in range(G):
                for dc in range(2):
                    nc.sync.wait_ge(vS.h, ytile[(tc, dc)])
                    oS.incd(nc.sync.dma_start(out=outr[half][:, tc, dc * 512:(dc + 1) * 512],
                                              in_=y_sb[:, tc, dc * 512:(dc + 1) * 512]))
            oS_done = oS.n

        nc.sync.wait_ge(oS.h, oS.n)
        for s_ in sems:
            nc.gpsimd.wait_ge(s_.h, s_.n)
        nc.all_engine_barrier()
    return nc


_NC_CACHE = {}


def _get_nc():
    if "nc" not in _NC_CACHE:
        _NC_CACHE["nc"] = build()
    return _NC_CACHE["nc"]


def _pm(wt):
    K, D = wt.shape
    return np.ascontiguousarray(wt.reshape(K // 128, 128, D).transpose(1, 0, 2))


def kernel(**inputs):
    f32 = np.float32
    cont = np.ascontiguousarray
    x = np.asarray(inputs["L0_attn_x"], f32)
    td = np.asarray(inputs["L0_attn_time_decay"], f32)
    tf = np.asarray(inputs["L0_attn_time_first"], f32)
    vw = np.asarray(inputs["L0_attn_value_w"], f32)
    rw = np.asarray(inputs["L0_attn_recep_w"], f32)
    ow = np.asarray(inputs["L0_attn_out_w"], f32)
    gw = np.asarray(inputs["L0_ffn_gate_w"], f32)
    gb = np.asarray(inputs["L0_ffn_gate_b"], f32)
    w1 = np.asarray(inputs["L0_ffn_w1"], f32)
    b1 = np.asarray(inputs["L0_ffn_b1"], f32)
    w3 = np.asarray(inputs["L0_ffn_w3"], f32)
    b3 = np.asarray(inputs["L0_ffn_b3"], f32)
    w2 = np.asarray(inputs["L0_ffn_w2"], f32)
    b2 = np.asarray(inputs["L0_ffn_b2"], f32)
    ws1 = np.asarray(inputs["L0_ffn_ws1"], f32)
    bs1 = np.asarray(inputs["L0_ffn_bs1"], f32)
    ws3 = np.asarray(inputs["L0_ffn_ws3"], f32)
    bs3 = np.asarray(inputs["L0_ffn_bs3"], f32)
    ws2 = np.asarray(inputs["L0_ffn_ws2"], f32)
    bs2 = np.asarray(inputs["L0_ffn_bs2"], f32)

    shared = dict(
        vwT=_pm(vw.T), rwT=_pm(rw.T), owT=_pm(ow.T), gwT=cont(gw.T),
        w1T=np.stack([_pm(w1[e].T) for e in range(E)]),
        w3T=np.stack([_pm(w3[e].T) for e in range(E)]),
        w2T=np.stack([_pm(w2[e].T) for e in range(E)]),
        ws1T=np.stack([_pm(ws1[k * INTER:(k + 1) * INTER].T) for k in range(2)]),
        ws3T=np.stack([_pm(ws3[k * INTER:(k + 1) * INTER].T) for k in range(2)]),
        ws2T=np.stack([_pm(ws2[:, k * INTER:(k + 1) * INTER].T) for k in range(2)]),
        b1h=cont(b1.reshape(E, G, 128).transpose(2, 0, 1)),
        b3h=cont(b3.reshape(E, G, 128).transpose(2, 0, 1)),
        b2g=cont(np.concatenate([b2, bs2[None, :]], 0)),
        bs1h=cont(bs1.reshape(2, G, 128).transpose(2, 0, 1)),
        bs3h=cont(bs3.reshape(2, G, 128).transpose(2, 0, 1)),
        gbc=cont(gb[:, None]),
        tdh=cont(td.reshape(G, 128).T), tfh=cont(tf.reshape(G, 128).T),
        idf=np.eye(128, dtype=f32),
        iotr=cont(np.broadcast_to(np.arange(RT - 1, -1, -1, dtype=f32)[None, :], (128, RT))),
    )
    in_maps = []
    for c in range(8):
        b_, j = c // 4, c % 4
        end = (j + 1) * OWN
        xsc = np.zeros((C, R), f32)
        xsc[:, R - end:] = x[b_, :end].T
        m = dict(shared)
        m["xs"] = cont(xsc.reshape(G, 128, NRT, RT).transpose(1, 2, 0, 3))
        m["xt"] = cont(x[b_, j * OWN:end])
        m["t0c"] = np.full((128, 1), float(j * OWN), f32)
        in_maps.append(m)

    nc = _get_nc()
    res = run_bass_kernel_spmd(nc, in_maps, core_ids=list(range(8)))
    outp = np.empty((2, 8192, C), f32)
    for c in range(8):
        b_, j = c // 4, c % 4
        outp[b_, j * OWN:(j + 1) * OWN] = np.asarray(res.results[c]["out"], f32)
    return outp


# version 30
# speedup vs baseline: 1.0304x; 1.0304x over previous
import numpy as np
import ml_dtypes as _mld
from contextlib import ExitStack
_bf16 = _mld.bfloat16

import concourse.bass as bass
import concourse.mybir as mybir
from concourse.bass_utils import run_bass_kernel_spmd

F32 = mybir.dt.float32
BF16 = mybir.dt.bfloat16
OP = mybir.AluOpType
AF = mybir.ActivationFunctionType
AX = mybir.AxisListType.X
ts = bass.ts

C = 1024
G = 8
R = 8192
OWN = 2048
RT = 256
NRT = R // RT
OWN_RT0 = (R - OWN) // RT
HALF = 1024
E = 8
INTER = 1024
SH = 2048


class Sem:
    def __init__(self, ctx, nc, name):
        self.h = ctx.enter_context(nc.semaphore(name))
        self.n = 0

    def inc(self, instr, k=1):
        instr.then_inc(self.h, k)
        self.n += k
        return instr

    def incd(self, instr):
        return self.inc(instr, 16)

    def wait(self, eng, val=None):
        eng.wait_ge(self.h, self.n if val is None else val)


def build():
    nc = bass.Bass("TRN2", target_bir_lowering=False)
    di = lambda n, s: nc.dram_tensor(n, s, F32, kind="ExternalInput").ap()
    xs = di("xs", [128, NRT, G, RT])
    xt = di("xt", [OWN, C])
    t0c = di("t0c", [128, 1])
    vwT = di("vwT", [128, G, C]); rwT = di("rwT", [128, G, C]); owT = di("owT", [128, G, C])
    gwT = di("gwT", [C, E])
    w1T = di("w1T", [E, 128, G, INTER]); w3T = di("w3T", [E, 128, G, INTER]); w2T = di("w2T", [E, 128, G, C])
    ws1T = di("ws1T", [2, 128, G, INTER]); ws3T = di("ws3T", [2, 128, G, INTER]); ws2T = di("ws2T", [2, 128, G, C])
    b1h = di("b1h", [128, E, G]); b3h = di("b3h", [128, E, G])
    b2g = di("b2g", [E + 1, C])
    bs1h = di("bs1h", [128, 2, G]); bs3h = di("bs3h", [128, 2, G])
    gbc = di("gbc", [E, 1])
    tdh = di("tdh", [128, G]); tfh = di("tfh", [128, G])
    idf = di("idf", [128, 128])
    iotr = di("iotr", [128, RT])
    out = nc.dram_tensor("out", [OWN, C], F32, kind="ExternalOutput").ap()

    with ExitStack() as ctx:
        sb = lambda n, s, d: ctx.enter_context(nc.sbuf_tensor(n, s, d))
        psum = lambda n, s: ctx.enter_context(nc.psum_tensor(n, s, F32))

        dS = Sem(ctx, nc, "dS")
        gD = Sem(ctx, nc, "gD")
        tS = Sem(ctx, nc, "tS")
        vS = Sem(ctx, nc, "vS")
        sS = Sem(ctx, nc, "sS")
        gS = Sem(ctx, nc, "gS")
        oS = Sem(ctx, nc, "oS")
        sems = [dS, gD, tS, vS, sS, gS, oS]

        for s_ in sems:
            nc.gpsimd.sem_clear(s_.h)
        nc.all_engine_barrier()

        lamc = sb("lamc", [128, G], F32)
        g1c = sb("g1c", [128, G], F32)
        e2c = sb("e2c", [128, G], F32)
        e2pc = sb("e2pc", [128, G], F32)
        b0c = sb("b0c", [128, G], F32)
        zeroc = sb("zeroc", [128, G], F32)
        onec = sb("onec", [128, 1], F32)
        epsc = sb("epsc", [128, 1], F32)
        ones_bf = sb("ones_bf", [128, 128], BF16)
        onesr_f = sb("onesr_f", [1, 128], F32)
        gbs = sb("gbs", [E, 1], F32)
        t0s = sb("t0s", [128, 1], F32)
        b1s = sb("b1s", [128, E, G], F32)
        b3s = sb("b3s", [128, E, G], F32)
        bs1s = sb("bs1s", [128, 2, G], F32)
        bs3s = sb("bs3s", [128, 2, G], F32)
        tdc = sb("tdc", [128, G], F32)
        tfc = sb("tfc", [128, G], F32)
        a_ = sb("a_", [128, G], F32)
        gx = sb("gx", [128, G], F32)
        scr = sb("scr", [128, G], F32)
        scr2 = sb("scr2", [128, G], F32)
        scr3 = sb("scr3", [128, G], F32)
        scr4 = sb("scr4", [128, G], F32)

        xn_own = sb("xn_own", [128, 8, G, RT], BF16)
        rwkv = sb("rwkv", [128, 8, G, RT], BF16)
        gtok = sb("gtok", [128, 16, E], F32)
        g9 = sb("g9", [E + 1, OWN], BF16)
        gws = sb("gws", [128, G, E], BF16)

        p1ctx = ExitStack()
        sb1 = lambda n, s, d: p1ctx.enter_context(nc.sbuf_tensor(n, s, d))
        vws = sb1("vws", [128, G, C], BF16)
        rws = sb1("rws", [128, G, C], BF16)

        ps = [psum(f"ps{i}", [128, 512]) for i in range(8)]
        bank_free = {}

        def bank_wait(i):
            if i in bank_free:
                s_, v_ = bank_free[i]
                nc.tensor.wait_ge(s_.h, v_)

        dS.incd(nc.sync.dma_start(out=tdc[:], in_=tdh[:, :]))
        dS.incd(nc.sync.dma_start(out=tfc[:], in_=tfh[:, :]))
        dS.incd(nc.sync.dma_start(out=t0s[:], in_=t0c[:, :]))
        dS.incd(nc.sync.dma_start(out=gbs[:], in_=gbc[:, :]))
        dS.incd(nc.sync.dma_start(out=b1s[:], in_=b1h[:, :, :]))
        dS.incd(nc.sync.dma_start(out=b3s[:], in_=b3h[:, :, :]))
        dS.incd(nc.sync.dma_start(out=bs1s[:], in_=bs1h[:, :, :]))
        dS.incd(nc.sync.dma_start(out=bs3s[:], in_=bs3h[:, :, :]))
        gD.incd(nc.gpsimd.dma_start(out=vws[:], in_=vwT[:, :, :]))
        gD.incd(nc.gpsimd.dma_start(out=rws[:], in_=rwT[:, :, :]))
        gD.incd(nc.gpsimd.dma_start(out=gws[:], in_=gwT.rearrange("(g p) e -> p g e", p=128)))

        nc.vector.wait_ge(dS.h, dS.n)
        nc.scalar.wait_ge(dS.h, dS.n)
        vS.inc(nc.vector.memset(zeroc[:], 0.0))
        vS.inc(nc.vector.memset(onec[:], 1.0))
        vS.inc(nc.vector.memset(epsc[:], 1e-5))
        vS.inc(nc.vector.memset(ones_bf[:], 1.0 / 1024.0))
        vS.inc(nc.vector.memset(onesr_f[:], 1.0))
        sS.inc(nc.scalar.activation(a_[:], tdc[:], AF.Relu))
        sS.inc(nc.scalar.activation(lamc[:], a_[:], AF.Exp, scale=-1.0))
        sS.inc(nc.scalar.activation(scr[:], tdc[:], AF.Relu, scale=-1.0))
        sS.inc(nc.scalar.activation(scr[:], scr[:], AF.Exp, scale=-1.0))
        sS.inc(nc.scalar.activation(scr2[:], tfc[:], AF.Relu))
        sS.inc(nc.scalar.activation(scr2[:], scr2[:], AF.Exp, scale=-1.0))
        sS.inc(nc.scalar.activation(e2c[:], tfc[:], AF.Relu, scale=-1.0))
        sS.inc(nc.scalar.activation(e2c[:], e2c[:], AF.Exp, scale=-1.0))
        sS.wait(nc.vector)
        vS.inc(nc.vector.tensor_mul(g1c[:], scr2[:], scr[:]))
        vS.inc(nc.vector.tensor_scalar(e2pc[:], e2c[:], 1.0, 1e-6, OP.mult, OP.add))

        TAU = 1.0 / 64.0
        BIG = float(2 ** 20)

        def g_of(dst, y, tmp1, tmp2):
            vS.inc(nc.vector.tensor_scalar(tmp1[:], y[:], 1.0 / 6.0, -0.5, OP.mult, OP.add))
            vS.inc(nc.vector.scalar_tensor_tensor(tmp1[:], y[:], 1.0, tmp1[:], OP.mult, OP.mult))
            vS.inc(nc.vector.tensor_scalar(tmp1[:], tmp1[:], 1.0, 1.0, OP.mult, OP.add))
            vS.inc(nc.vector.tensor_scalar(tmp2[:], y[:], TAU / 2.0, TAU / 2.0, OP.max, OP.max))
            vS.inc(nc.vector.reciprocal(dst[:], tmp2[:]))
            vS.wait(nc.scalar)
            sS.inc(nc.scalar.activation(tmp2[:], tmp2[:], AF.Exp, scale=-1.0))
            sS.wait(nc.vector)
            vS.inc(nc.vector.tensor_scalar(tmp2[:], tmp2[:], -1.0, 1.0, OP.mult, OP.add))
            vS.inc(nc.vector.tensor_mul(dst[:], dst[:], tmp2[:]))
            vS.inc(nc.vector.tensor_scalar(tmp2[:], y[:], -BIG, TAU * BIG, OP.mult, OP.add))
            vS.inc(nc.vector.tensor_scalar(tmp2[:], tmp2[:], 0.0, 1.0, OP.max, OP.min))
            vS.inc(nc.vector.tensor_sub(tmp1[:], tmp1[:], dst[:]))
            vS.inc(nc.vector.tensor_mul(tmp1[:], tmp1[:], tmp2[:]))
            vS.inc(nc.vector.tensor_add(dst[:], dst[:], tmp1[:]))

        sS.wait(nc.vector)
        vS.inc(nc.vector.tensor_scalar_mul(scr3[:], a_[:], t0s[:, 0:1]))
        g_of(scr4, scr3, scr, scr2)
        vS.inc(nc.vector.tensor_copy(gx[:], scr4[:]))
        g_of(scr4, a_, scr, scr2)
        vS.inc(nc.vector.reciprocal(scr4[:], scr4[:]))
        vS.inc(nc.vector.tensor_mul(gx[:], gx[:], scr4[:]))
        vS.inc(nc.vector.tensor_scalar_mul(b0c[:], gx[:], t0s[:, 0:1]))

        nc.all_engine_barrier()

        xf = [sb1(f"xf{i}", [128, G, RT], F32) for i in range(2)]
        xsq = [sb1(f"xsq{i}", [128, 2, G, RT], BF16) for i in range(2)]
        v_sb = [sb1(f"v_sb{i}", [128, G, RT], F32) for i in range(2)]
        r_sb = sb1("r_sb", [128, G, RT], F32)
        aa = sb1("aa", [128, G, RT + 1], F32)
        bb = sb1("bb", [128, G, RT + 1], F32)
        aaS = sb1("aaS", [128, G], F32)
        bbS = sb1("bbS", [128, G], F32)
        num = sb1("num", [128, G, RT], F32)
        den = sb1("den", [128, G, RT], F32)
        wlam = sb1("wlam", [128, G, RT], F32)
        lamRT = sb1("lamRT", [128, G], F32)
        nega = sb1("nega", [128, G], F32)
        ctrb = sb1("ctrb", [128, G], F32)
        is_ = [sb1(f"is{i}", [128, RT], F32) for i in range(3)]
        msi = [sb1(f"msi{i}", [128, RT], F32) for i in range(3)]

        vS.inc(nc.vector.tensor_copy(aaS[:], zeroc[:]))
        vS.inc(nc.vector.tensor_copy(bbS[:], b0c[:]))
        dS.incd(nc.sync.dma_start(out=is_[0][:], in_=iotr[:, :]))
        vS.inc(nc.vector.tensor_scalar(nega[:], a_[:], -1.0, 0.0, OP.mult, OP.add))
        nc.scalar.wait_ge(dS.h, dS.n)
        sS.wait(nc.scalar)
        vS.wait(nc.scalar)
        for g in range(G):
            sS.inc(nc.scalar.activation(wlam[:, g, :], is_[0][:], AF.Exp, scale=nega[:, g:g + 1]))
        sS.inc(nc.scalar.activation(lamRT[:], a_[:], AF.Exp, scale=-float(RT)))
        sS.wait(nc.vector)

        xf_free = [None, None]
        guards = {}
        LN = {}
        pend_rwkv = []

        def emit_LN(rt):
            par = rt % 3
            fpar = rt % 2
            xpar = rt % 2
            own = rt >= OWN_RT0
            o0 = (rt - OWN_RT0) * RT
            if xf_free[fpar] is not None:
                s_, v_ = xf_free[fpar]
                nc.sync.wait_ge(s_.h, v_)
            dS.incd(nc.sync.dma_start(out=xf[fpar][:], in_=xs[:, rt]))
            dma_done = dS.n
            nc.scalar.wait_ge(dS.h, dma_done)
            if ("stats_mm", xpar) in guards:
                nc.scalar.wait_ge(tS.h, guards[("stats_mm", xpar)])
            sS.inc(nc.scalar.copy(xsq[xpar][:, 0], xf[fpar][:]))
            cast_done = sS.n
            nc.scalar.wait_ge(dS.h, dma_done)
            if ("stats_mm", xpar) in guards:
                nc.scalar.wait_ge(tS.h, guards[("stats_mm", xpar)])
            sS.inc(nc.scalar.activation(xsq[xpar][:, 1], xf[fpar][:], AF.Square))
            sq_done = sS.n
            bank_wait(4 + par)
            nc.tensor.wait_ge(sS.h, max(cast_done, sq_done))
            for g in range(G):
                mm = nc.tensor.matmul(ps[4 + par][:, 0:2 * RT], ones_bf[:], xsq[xpar][:, :, g, :],
                                      start=(g == 0), stop=(g == G - 1))
            tS.inc(mm)
            guards[("stats_mm", xpar)] = tS.n
            bcast_done = tS.n
            if ("u_mult", par) in guards:
                nc.scalar.wait_ge(vS.h, guards[("u_mult", par)])
            nc.scalar.wait_ge(tS.h, bcast_done)
            sS.inc(nc.scalar.activation(is_[par][:], ps[4 + par][:, 0:RT], AF.Square))
            msq_done = sS.n
            nc.vector.wait_ge(sS.h, msq_done)
            vS.inc(nc.vector.tensor_sub(msi[par][:], ps[4 + par][:, RT:2 * RT], is_[par][:]))
            var_done = vS.n
            nc.scalar.wait_ge(vS.h, var_done)
            sS.inc(nc.scalar.activation(is_[par][:], msi[par][:], AF.Ln, bias=epsc[:, 0:1]))
            sS.inc(nc.scalar.activation(is_[par][:], is_[par][:], AF.Exp, scale=-0.5))
            is_done = sS.n
            nc.vector.wait_ge(sS.h, is_done)
            vS.inc(nc.vector.tensor_mul(msi[par][:], ps[4 + par][:, 0:RT], is_[par][:]))
            bank_free[4 + par] = (vS, vS.n)
            vS.inc(nc.vector.tensor_mul(
                xf[fpar][:], xf[fpar][:],
                is_[par].rearrange("p (u t) -> p u t", u=1).broadcast_to([128, G, RT])))
            vS.inc(nc.vector.tensor_sub(
                xf[fpar][:], xf[fpar][:],
                msi[par].rearrange("p (u t) -> p u t", u=1).broadcast_to([128, G, RT])))
            guards[("u_mult", par)] = vS.n
            guards[("xn", par)] = vS.n
            u_done = vS.n
            slot = rt % 8
            xn_rt = xn_own[:, slot]
            nc.gpsimd.wait_ge(vS.h, u_done)
            if ("xn_slot", slot) in guards:
                nc.gpsimd.wait_ge(tS.h, guards[("xn_slot", slot)])
            gD.incd(nc.gpsimd.dma_start(out=xn_rt, in_=xf[fpar][:]))
            xf_free[fpar] = (gD, gD.n)
            LN[rt] = dict(xn=(gD, gD.n), xn_rt=xn_rt)

        def emit_C(rt):
            par = rt % 2
            own = rt >= OWN_RT0
            o0 = (rt - OWN_RT0) * RT
            xn_rt = LN[rt]["xn_rt"]
            while pend_rwkv:
                v_, oo = pend_rwkv.pop(0)
                nc.gpsimd.wait_ge(vS.h, v_)
                gD.incd(nc.gpsimd.dma_start(out=rwkv[:, oo // RT], in_=num[:]))
                guards["rwkv_cast"] = gD.n
            xnsem, xnval = LN[rt]["xn"]
            nc.tensor.wait_ge(xnsem.h, xnval)
            for dg in range(G):
                b = dg % 4
                bank_wait(b)
                for kg in range(G):
                    mm = nc.tensor.matmul(ps[b][:, 0:RT], vws[:, kg, ts(dg, 128)], xn_rt[:, kg, :],
                                          start=(kg == 0), stop=(kg == G - 1))
                tS.inc(mm)
                if ("scan", par) in guards:
                    nc.scalar.wait_ge(vS.h, guards[("scan", par)])
                tS.wait(nc.scalar)
                sS.inc(nc.scalar.copy(v_sb[par][:, dg, :], ps[b][:, 0:RT]))
                bank_free[b] = (sS, sS.n)
            v_evac_done = sS.n
            guards[("xn_slot", rt % 8)] = tS.n
            nc.vector.wait_ge(sS.h, v_evac_done)
            if own:
                for g in range(G):
                    vS.inc(nc.vector.tensor_copy(aa[:, g, 0:1], aaS[:, g:g + 1]))
                for g in range(G):
                    vS.inc(nc.vector.tensor_tensor_scan(
                        aa[:, g, 1:RT + 1], lamc[:, g:g + 1].broadcast_to([128, RT]), v_sb[par][:, g, :],
                        aaS[:, g:g + 1], OP.mult, OP.add))
                for g in range(G):
                    vS.inc(nc.vector.tensor_copy(aaS[:, g:g + 1], aa[:, g, RT:RT + 1]))
            else:
                for g in range(G):
                    vS.inc(nc.vector.scalar_tensor_tensor(
                        aa[:, g, 0:RT], v_sb[par][:, g, :], 1.0, wlam[:, g, :],
                        OP.mult, OP.mult, accum_out=ctrb[:, g:g + 1]))
                vS.inc(nc.vector.tensor_mul(aaS[:], aaS[:], lamRT[:]))
                vS.inc(nc.vector.tensor_add(aaS[:], aaS[:], ctrb[:]))
            guards[("scan", par)] = vS.n
            if own:
                for g in range(G):
                    vS.inc(nc.vector.tensor_copy(bb[:, g, 0:1], bbS[:, g:g + 1]))
                for g in range(G):
                    vS.inc(nc.vector.tensor_tensor_scan(
                        bb[:, g, 1:RT + 1], lamc[:, g:g + 1].broadcast_to([128, RT]),
                        onec[:, 0:1].broadcast_to([128, RT]), bbS[:, g:g + 1], OP.mult, OP.add))
                for g in range(G):
                    vS.inc(nc.vector.tensor_copy(bbS[:, g:g + 1], bb[:, g, RT:RT + 1]))
                guards[("scan", par)] = vS.n
                for dg in range(G):
                    b = dg % 4
                    bank_wait(b)
                    for kg in range(G):
                        mm = nc.tensor.matmul(ps[b][:, 0:RT], rws[:, kg, ts(dg, 128)],
                                              xn_own[:, rt - OWN_RT0, kg, :],
                                              start=(kg == 0), stop=(kg == G - 1))
                    tS.inc(mm)
                    if "rmult" in guards:
                        nc.scalar.wait_ge(vS.h, guards["rmult"])
                    tS.wait(nc.scalar)
                    sS.inc(nc.scalar.activation(r_sb[:, dg, :], ps[b][:, 0:RT], AF.Exp, scale=-1.0))
                    bank_free[b] = (sS, sS.n)
                sS.inc(nc.scalar.activation(r_sb[:], r_sb[:], AF.Ln, bias=onec[:, 0:1]))
                sS.inc(nc.scalar.activation(r_sb[:], r_sb[:], AF.Exp, scale=-1.0))
                r_done = sS.n
                if "wkv_mult" in guards:
                    nc.scalar.wait_ge(vS.h, guards["wkv_mult"])
                for g in range(G):
                    sS.inc(nc.scalar.mul(den[:, g, :], v_sb[par][:, g, :], e2c[:, g:g + 1]))
                e2v_done = sS.n
                if "rwkv_cast" in guards:
                    nc.vector.wait_ge(gD.h, guards["rwkv_cast"])
                nc.vector.wait_ge(sS.h, e2v_done)
                for g in range(G):
                    vS.inc(nc.vector.scalar_tensor_tensor(
                        num[:, g, :], aa[:, g, 0:RT], g1c[:, g:g + 1], den[:, g, :], OP.mult, OP.add))
                    vS.inc(nc.vector.tensor_scalar(
                        den[:, g, :], bb[:, g, 0:RT], g1c[:, g:g + 1], e2pc[:, g:g + 1], OP.mult, OP.add))
                den_done = vS.n
                nc.scalar.wait_ge(vS.h, den_done)
                sS.inc(nc.scalar.activation(den[:], den[:], AF.Ln))
                sS.inc(nc.scalar.activation(den[:], den[:], AF.Exp, scale=-1.0))
                rden_done = sS.n
                nc.vector.wait_ge(sS.h, rden_done)
                nc.vector.wait_ge(sS.h, r_done)
                vS.inc(nc.vector.tensor_mul(num[:], num[:], den[:]))
                guards["wkv_mult"] = vS.n
                vS.inc(nc.vector.tensor_mul(num[:], num[:], r_sb[:]))
                guards["rmult"] = vS.n
                pend_rwkv.append((vS.n, o0))

        emit_LN(0)
        emit_LN(1)
        for rt in range(NRT):
            if rt + 2 < NRT:
                emit_LN(rt + 2)
            emit_C(rt)
        while pend_rwkv:
            v_, oo = pend_rwkv.pop(0)
            nc.gpsimd.wait_ge(vS.h, v_)
            gD.incd(nc.gpsimd.dma_start(out=rwkv[:, oo // RT], in_=num[:]))
        nc.gpsimd.wait_ge(gD.h, gD.n)

        nc.all_engine_barrier()
        bank_free.clear()
        p1ctx.close()

        ows = sb("ows", [128, G, C], BF16)
        b2gs = sb("b2gs", [E + 1, C], BF16)
        gD.incd(nc.gpsimd.dma_start(out=ows[:], in_=owT[:, :, :]))
        gD.incd(nc.gpsimd.dma_start(out=b2gs[:], in_=b2g[:, :]))

        y_sb = sb("y_sb", [128, G, C], F32)
        w1s = sb("w1s", [128, G, INTER], BF16)
        w3s = sb("w3s", [128, G, INTER], BF16)
        w2s = sb("w2s", [128, G, C], BF16)
        w1r = [w1T[e] for e in range(E)]
        w3r = [w3T[e] for e in range(E)]
        w2r = [w2T[e] for e in range(E)]
        gD.incd(nc.gpsimd.dma_start(out=w1s[:], in_=w1r[0]))
        gD.incd(nc.gpsimd.dma_start(out=w3s[:], in_=w3r[0]))
        gD.incd(nc.gpsimd.dma_start(out=w2s[:], in_=w2r[0]))
        guards["p2w"] = gD.n

        gctx = ExitStack()
        sbg = lambda n, s, d: gctx.enter_context(nc.sbuf_tensor(n, s, d))
        idsb = sbg("idsb", [128, 128], F32)
        dS.incd(nc.sync.dma_start(out=idsb[:], in_=idf[:, :]))
        nc.tensor.wait_ge(dS.h, dS.n)
        lgf = sbg("lgf", [E, OWN], F32)
        pall = sbg("pall", [128, 16, E], F32)
        racc = sbg("racc", [128, 16, 32], F32)
        rsum = sbg("rsum", [128, 16], F32)
        colmx = sbg("colmx", [128, 1], F32)
        lgt_sb = sbg("lgt_sb", [128, E], F32)
        colsum = sbg("colsum", [128, 1], F32)
        m2t = sbg("m2t", [128, 16], F32)
        dsc = sbg("dsc", [128, 16, E], F32)

        for nchunk in range(4):
            b = nchunk % 2
            bank_wait(b)
            for kg in range(G):
                mm = nc.tensor.matmul(ps[b][0:E, 0:512], gws[:, kg, :],
                                      xn_own[:, 2 * nchunk:2 * nchunk + 2, kg, :],
                                      start=(kg == 0), stop=(kg == G - 1))
            tS.inc(mm)
            tS.wait(nc.scalar)
            sS.inc(nc.scalar.activation(lgf[:, nchunk * 512:(nchunk + 1) * 512], ps[b][0:E, 0:512],
                                        AF.Identity, bias=gbs[:, 0:1]))
            bank_free[b] = (sS, sS.n)
        for tcg in range(16):
            b = 2 + tcg % 2
            bank_wait(b)
            sS.wait(nc.tensor)
            tS.inc(nc.tensor.transpose(ps[b][:, 0:E], lgf[:, tcg * 128:(tcg + 1) * 128], idsb[0:E, 0:E]))
            tS.wait(nc.scalar)
            sS.inc(nc.scalar.copy(lgt_sb[:], ps[b][:, 0:E]))
            sS.wait(nc.vector)
            vS.inc(nc.vector.reduce_max(colmx[:], lgt_sb[:], axis=AX, negate=True))
            vS.wait(nc.scalar)
            sS.inc(nc.scalar.activation(pall[:, tcg, :], ps[b][:, 0:E], AF.Exp,
                                        bias=colmx[:, 0:1], accum_out=colsum[:, 0:1]))
            bank_free[b] = (sS, sS.n)
            sS.wait(nc.vector)
            vS.inc(nc.vector.reciprocal(rsum[:, tcg:tcg + 1], colsum[:, 0:1]))
        for k in range(1, 5):
            vS.inc(nc.vector.tensor_tensor(racc[:, :, (k - 1) * 8:(k - 1) * 8 + (8 - k)],
                                           pall[:, :, 0:8 - k], pall[:, :, k:8], OP.min))
            vS.inc(nc.vector.tensor_tensor(racc[:, :, (k - 1) * 8 + (8 - k):k * 8],
                                           pall[:, :, 8 - k:8], pall[:, :, 0:k], OP.min))
        vS.inc(nc.vector.reduce_max(m2t.rearrange("p (c u) -> p c u", u=1), racc[:], axis=AX))
        vS.inc(nc.vector.memset(g9[:, :], 1.0))
        vS.inc(nc.vector.tensor_sub(dsc[:], pall[:],
                                    m2t.rearrange("p (c u) -> p c u", u=1).broadcast_to([128, 16, E])))
        vS.inc(nc.vector.tensor_scalar(dsc[:], dsc[:], 1.0, 1e-12, OP.mult, OP.add))
        vS.wait(nc.scalar)
        sS.inc(nc.scalar.sign(dsc[:], dsc[:]))
        sS.wait(nc.vector)
        vS.inc(nc.vector.tensor_scalar(dsc[:], dsc[:], 0.0, 0.0, OP.max, OP.max))
        vS.inc(nc.vector.tensor_mul(gtok[:], pall[:], dsc[:]))
        vS.inc(nc.vector.tensor_mul(gtok[:], gtok[:],
                                    rsum.rearrange("p (c u) -> p c u", u=1).broadcast_to([128, 16, E])))
        vS.wait(nc.scalar)
        for tcg in range(16):
            b = 2 + tcg % 2
            bank_wait(b)
            vS.wait(nc.tensor)
            tS.inc(nc.tensor.transpose(ps[b][0:E, 0:128], gtok[:, tcg, :], idsb[:, :]))
            tS.wait(nc.scalar)
            sS.inc(nc.scalar.copy(g9[0:E, tcg * 128:(tcg + 1) * 128], ps[b][0:E, 0:128]))
            bank_free[b] = (sS, sS.n)

        nc.all_engine_barrier()
        bank_free.clear()
        gctx.close()


        xtr = xt.rearrange("(h c p) d -> h p c d", p=128, c=G)
        outr = out.rearrange("(h c p) d -> h p c d", p=128, c=G)

        ws1r = [ws1T[k] for k in range(2)]
        ws3r = [ws3T[k] for k in range(2)]
        ws2r = [ws2T[k] for k in range(2)]

        contribs = [("exp", e) for e in range(E)] + [("sh", 0), ("sh", 1)]
        w13_ready = guards.pop("p2w", gD.n)
        w2_ready = w13_ready
        slot_free = {0: 0, 1: 0}

        def tile_iter(wave):
            for i in range(8):
                yield i, wave * 4 + i // 2, i % 2

        for half in range(2):
            tbase = half * HALF
            if half == 1:
                nc.sync.wait_ge(tS.h, tS.n)
                nc.vector.wait_ge(oS.h, oS_done)
            hctx = ExitStack()
            xts = hctx.enter_context(nc.sbuf_tensor(f"xts{half}", [128, G, C], F32))
            if half == 1:
                nc.gpsimd.wait_ge(tS.h, tS.n)
            gD.incd(nc.gpsimd.dma_start(out=xts[:], in_=xtr[half]))

            nc.tensor.wait_ge(gD.h, max(w13_ready, gD.n))
            for wave in range(2):
                for i, tc, dc in tile_iter(wave):
                    bank_wait(i)
                    tS.inc(nc.tensor.matmul(ps[i][:, :], g9[:, tbase + tc * 128:tbase + (tc + 1) * 128],
                                            b2gs[:, dc * 512:(dc + 1) * 512], start=True, stop=True))
                tS.wait(nc.vector)
                for i, tc, dc in tile_iter(wave):
                    vS.inc(nc.vector.tensor_add(y_sb[:, tc, dc * 512:(dc + 1) * 512],
                                                ps[i][:, :], xts[:, tc, dc * 512:(dc + 1) * 512]))
                    bank_free[i] = (vS, vS.n)

            for wave in range(2):
                for i, tc, dc in tile_iter(wave):
                    bank_wait(i)
                    gtc = half * 8 + tc
                    for kg in range(G):
                        mm = nc.tensor.matmul(ps[i][:, :], rwkv[:, gtc // 2, kg, (gtc % 2) * 128:(gtc % 2 + 1) * 128],
                                              ows[:, kg, dc * 512:(dc + 1) * 512],
                                              start=(kg == 0), stop=(kg == G - 1))
                    tS.inc(mm)
                tS.wait(nc.vector)
                for i, tc, dc in tile_iter(wave):
                    ysl = y_sb[:, tc, dc * 512:(dc + 1) * 512]
                    vS.inc(nc.vector.tensor_tensor(ysl, ps[i][:, :], ysl, OP.add))
                    bank_free[i] = (vS, vS.n)

            hctx.close()
            hctx = ExitStack()
            h_bf = hctx.enter_context(nc.sbuf_tensor(f"h_bf{half}", [128, G, HALF], BF16))
            a1s = hctx.enter_context(nc.sbuf_tensor(f"a1s{half}", [128, 2, 512], F32))
            a3s = hctx.enter_context(nc.sbuf_tensor(f"a3s{half}", [128, 2, 512], F32))
            ytile = {}
            for ci, (kind, e) in enumerate(contribs):
                nc.tensor.wait_ge(gD.h, w13_ready)
                for ig in range(G):
                    for tch in range(2):
                        b1_ = tch
                        b3_ = 2 + tch
                        bank_wait(b1_)
                        bank_wait(b3_)
                        sl0 = 4 * half + 2 * tch
                        for kg in range(G):
                            nc.tensor.matmul(ps[b1_][:, :], w1s[:, kg, ts(ig, 128)],
                                             xn_own[:, sl0:sl0 + 2, kg, :],
                                             start=(kg == 0), stop=(kg == G - 1))
                        for kg in range(G):
                            mm = nc.tensor.matmul(ps[b3_][:, :], w3s[:, kg, ts(ig, 128)],
                                                  xn_own[:, sl0:sl0 + 2, kg, :],
                                                  start=(kg == 0), stop=(kg == G - 1))
                        tS.inc(mm)
                        if kind == "exp":
                            bc1 = b1s[:, e, ig:ig + 1]
                            bc3 = b3s[:, e, ig:ig + 1]
                        else:
                            bc1 = bs1s[:, e, ig:ig + 1]
                            bc3 = bs3s[:, e, ig:ig + 1]
                        nc.scalar.wait_ge(gS.h, slot_free[tch])
                        tS.wait(nc.scalar)
                        sS.inc(nc.scalar.activation(a1s[:, tch, :], ps[b1_][:, :], AF.Silu, bias=bc1))
                        sS.inc(nc.scalar.activation(a3s[:, tch, :], ps[b3_][:, :], AF.Identity, bias=bc3))
                        bank_free[b1_] = (sS, sS.n)
                        bank_free[b3_] = (sS, sS.n)
                        sS.wait(nc.gpsimd)
                        gS.inc(nc.gpsimd.tensor_tensor(h_bf[:, ig, tch * 512:(tch + 1) * 512],
                                                       a1s[:, tch, :], a3s[:, tch, :], OP.mult))
                        slot_free[tch] = gS.n
                tS.wait(nc.gpsimd)
                if ci + 1 < len(contribs):
                    nk, ne = contribs[ci + 1]
                    gD.incd(nc.gpsimd.dma_start(out=w1s[:], in_=(w1r[ne] if nk == "exp" else ws1r[ne])))
                    gD.incd(nc.gpsimd.dma_start(out=w3s[:], in_=(w3r[ne] if nk == "exp" else ws3r[ne])))
                elif half == 0:
                    gD.incd(nc.gpsimd.dma_start(out=w1s[:], in_=w1r[0]))
                    gD.incd(nc.gpsimd.dma_start(out=w3s[:], in_=w3r[0]))
                w13_ready = gD.n
                nc.tensor.wait_ge(gS.h, gS.n)
                nc.tensor.wait_ge(gD.h, w2_ready)
                for wave in range(2):
                    for i, tc, dc in tile_iter(wave):
                        bank_wait(i)
                        for kg in range(G):
                            mm = nc.tensor.matmul(ps[i][:, :], h_bf[:, kg, tc * 128:(tc + 1) * 128],
                                                  w2s[:, kg, dc * 512:(dc + 1) * 512],
                                                  start=(kg == 0), stop=(kg == G - 1))
                        tS.inc(mm)
                    tS.wait(nc.vector)
                    for i, tc, dc in tile_iter(wave):
                        ysl = y_sb[:, tc, dc * 512:(dc + 1) * 512]
                        if kind == "exp":
                            gcol = gtok[:, half * 8 + tc, e:e + 1]
                            vS.inc(nc.vector.scalar_tensor_tensor(ysl, ps[i][:, :], gcol, ysl, OP.mult, OP.add))
                        else:
                            vS.inc(nc.vector.tensor_tensor(ysl, ps[i][:, :], ysl, OP.add))
                        bank_free[i] = (vS, vS.n)
                        if ci == len(contribs) - 1:
                            ytile[(tc, dc)] = vS.n
                tS.wait(nc.gpsimd)
                if ci + 1 < len(contribs):
                    nk, ne = contribs[ci + 1]
                    gD.incd(nc.gpsimd.dma_start(out=w2s[:], in_=(w2r[ne] if nk == "exp" else ws2r[ne])))
                elif half == 0:
                    gD.incd(nc.gpsimd.dma_start(out=w2s[:], in_=w2r[0]))
                w2_ready = gD.n

            hctx.close()
            for tc in range(G):
                for dc in range(2):
                    nc.sync.wait_ge(vS.h, ytile[(tc, dc)])
                    oS.incd(nc.sync.dma_start(out=outr[half][:, tc, dc * 512:(dc + 1) * 512],
                                              in_=y_sb[:, tc, dc * 512:(dc + 1) * 512]))
            oS_done = oS.n

        nc.sync.wait_ge(oS.h, oS.n)
        for s_ in sems:
            nc.gpsimd.wait_ge(s_.h, s_.n)
        nc.all_engine_barrier()
    return nc


_NC_CACHE = {}


def _get_nc():
    if "nc" not in _NC_CACHE:
        _NC_CACHE["nc"] = build()
    return _NC_CACHE["nc"]


def _pm(wt):
    K, D = wt.shape
    return np.ascontiguousarray(wt.reshape(K // 128, 128, D).transpose(1, 0, 2))


def kernel(**inputs):
    f32 = np.float32
    cont = np.ascontiguousarray
    x = np.asarray(inputs["L0_attn_x"], f32)
    td = np.asarray(inputs["L0_attn_time_decay"], f32)
    tf = np.asarray(inputs["L0_attn_time_first"], f32)
    vw = np.asarray(inputs["L0_attn_value_w"], f32)
    rw = np.asarray(inputs["L0_attn_recep_w"], f32)
    ow = np.asarray(inputs["L0_attn_out_w"], f32)
    gw = np.asarray(inputs["L0_ffn_gate_w"], f32)
    gb = np.asarray(inputs["L0_ffn_gate_b"], f32)
    w1 = np.asarray(inputs["L0_ffn_w1"], f32)
    b1 = np.asarray(inputs["L0_ffn_b1"], f32)
    w3 = np.asarray(inputs["L0_ffn_w3"], f32)
    b3 = np.asarray(inputs["L0_ffn_b3"], f32)
    w2 = np.asarray(inputs["L0_ffn_w2"], f32)
    b2 = np.asarray(inputs["L0_ffn_b2"], f32)
    ws1 = np.asarray(inputs["L0_ffn_ws1"], f32)
    bs1 = np.asarray(inputs["L0_ffn_bs1"], f32)
    ws3 = np.asarray(inputs["L0_ffn_ws3"], f32)
    bs3 = np.asarray(inputs["L0_ffn_bs3"], f32)
    ws2 = np.asarray(inputs["L0_ffn_ws2"], f32)
    bs2 = np.asarray(inputs["L0_ffn_bs2"], f32)

    shared = dict(
        vwT=_pm(vw.T), rwT=_pm(rw.T), owT=_pm(ow.T), gwT=cont(gw.T),
        w1T=np.stack([_pm(w1[e].T) for e in range(E)]),
        w3T=np.stack([_pm(w3[e].T) for e in range(E)]),
        w2T=np.stack([_pm(w2[e].T) for e in range(E)]),
        ws1T=np.stack([_pm(ws1[k * INTER:(k + 1) * INTER].T) for k in range(2)]),
        ws3T=np.stack([_pm(ws3[k * INTER:(k + 1) * INTER].T) for k in range(2)]),
        ws2T=np.stack([_pm(ws2[:, k * INTER:(k + 1) * INTER].T) for k in range(2)]),
        b1h=cont(b1.reshape(E, G, 128).transpose(2, 0, 1)),
        b3h=cont(b3.reshape(E, G, 128).transpose(2, 0, 1)),
        b2g=cont(np.concatenate([b2, bs2[None, :]], 0)),
        bs1h=cont(bs1.reshape(2, G, 128).transpose(2, 0, 1)),
        bs3h=cont(bs3.reshape(2, G, 128).transpose(2, 0, 1)),
        gbc=cont(gb[:, None]),
        tdh=cont(td.reshape(G, 128).T), tfh=cont(tf.reshape(G, 128).T),
        idf=np.eye(128, dtype=f32),
        iotr=cont(np.broadcast_to(np.arange(RT - 1, -1, -1, dtype=f32)[None, :], (128, RT))),
    )
    in_maps = []
    for c in range(8):
        b_, j = c // 4, c % 4
        end = (j + 1) * OWN
        xsc = np.zeros((C, R), f32)
        xsc[:, R - end:] = x[b_, :end].T
        m = dict(shared)
        m["xs"] = cont(xsc.reshape(G, 128, NRT, RT).transpose(1, 2, 0, 3))
        m["xt"] = cont(x[b_, j * OWN:end])
        m["t0c"] = np.full((128, 1), float(j * OWN), f32)
        in_maps.append(m)

    nc = _get_nc()
    res = run_bass_kernel_spmd(nc, in_maps, core_ids=list(range(8)))
    outp = np.empty((2, 8192, C), f32)
    for c in range(8):
        b_, j = c // 4, c % 4
        outp[b_, j * OWN:(j + 1) * OWN] = np.asarray(res.results[c]["out"], f32)
    return outp
